# version 01
# baseline (speedup 1.0000x reference)
import numpy as np
import ml_dtypes
from contextlib import ExitStack

import concourse.bass as bass
import concourse.mybir as mybir
from concourse.bass_utils import run_bass_kernel_spmd

F32 = mybir.dt.float32
BF16 = mybir.dt.bfloat16
I32 = mybir.dt.int32
U32 = mybir.dt.uint32
AF = mybir.ActivationFunctionType
ALU = mybir.AluOpType
AX = mybir.AxisListType
BF = ml_dtypes.bfloat16

B, S, D = 4, 4096, 1024
NCORES = 8
TOK = 2048
NT = TOK // 128
NGT = S // 128
FCH = D // 128
LN_EPS = 1e-5
DEBUG_LN_BYPASS = False
ATTN_STAGE = 99
DEBUG_DUMP_OSB = None
DEBUG_SKIP_SIB0_PV = False
DEBUG_SKIP_PROLOGUE = False


def own_tiles(h):
    return [2 * i + h for i in range(NT)]


def shard_x(x_full_batch, h):
    t = np.ascontiguousarray(x_full_batch).reshape(NGT, 128, D)
    return np.ascontiguousarray(t[own_tiles(h)].reshape(TOK, D))


def unshard_x(x_even, x_odd):
    out = np.empty((NGT, 128, D), x_even.dtype)
    out[0::2] = x_even.reshape(NT, 128, D)
    out[1::2] = x_odd.reshape(NT, 128, D)
    return out.reshape(S, D)


class Prog:
    def __init__(self, nc, ctx):
        self.nc = nc
        self.ctx = ctx
        self._sems = {}
        self._cnt = {}

    def sem(self, name):
        if name not in self._sems:
            self._sems[name] = self.ctx.enter_context(self.nc.semaphore(name))
            self._cnt[name] = 0
        return self._sems[name]

    def inc(self, instr, name, n=1):
        instr.then_inc(self.sem(name), n)
        self._cnt[name] += n
        return self._cnt[name]

    def cnt(self, name):
        return self._cnt.get(name, 0)

    def wait(self, eng, name, tok):
        if tok > 0:
            eng.wait_ge(self.sem(name), tok)

    def sbuf(self, name, shape, dtype):
        return self.ctx.enter_context(self.nc.sbuf_tensor(name, shape, dtype))

    def psum(self, name, shape, dtype=F32):
        return self.ctx.enter_context(self.nc.psum_tensor(name, shape, dtype))

    def dram(self, name, shape, dtype, **kw):
        return self.nc.dram_tensor(name, shape, dtype, **kw)

    def dma(self, eng, stream, out, in_, **kw):
        ins = eng.dma_start(out=out, in_=in_, **kw)
        return self.inc(ins, "D_" + stream, 16)

    def wait_dma(self, eng, stream):
        c = self.cnt("D_" + stream)
        if c > 0:
            eng.wait_ge(self.sem("D_" + stream), c)

    def barrier(self, engines, sems=None):
        for eng in engines:
            for sname, c in list(self._cnt.items()):
                if c > 0:
                    eng.wait_ge(self._sems[sname], c)


def xlnT_col(fc, i0, n):
    return slice(fc * TOK + i0, fc * TOK + i0 + n)


def emit_ln_xlnt(p, xres, xlnT, bufs, tag):
    nc = p.nc
    v, sc, te = nc.vector, nc.scalar, nc.tensor
    stats = bufs["stats"]
    sq_dump = bufs["sq_dump"]
    xln_tok = bufs["xln_tok"]
    tp_ps = bufs["tp_ps"]
    ident = bufs["ident"]

    ssum = stats[:, 0:16]
    ssq = stats[:, 16:32]
    nm = stats[:, 32:48]
    rs = stats[:, 48:64]
    m2 = stats[:, 64:80]

    stok = 0
    for i in range(NT):
        xt = xres[:, i * 1024:(i + 1) * 1024]
        p.inc(v.tensor_reduce(stats[:, i:i + 1], xt, AX.X, ALU.add), "vec")
        stok = p.inc(sc.activation(sq_dump[:, :], xt, AF.Square,
                                   accum_out=stats[:, 16 + i:17 + i]), "act")
    p.wait(v, "act", stok)
    v.drain()
    p.inc(v.tensor_scalar(nm, ssum, -1.0 / D, None, ALU.mult), "vec")
    p.inc(v.tensor_scalar(ssq, ssq, 1.0 / D, None, ALU.mult), "vec")
    v.drain()
    p.inc(v.tensor_tensor(m2, nm, nm, ALU.mult), "vec")
    v.drain()
    vt = p.inc(v.tensor_tensor(ssq, ssq, m2, ALU.subtract), "vec")
    p.wait(sc, "vec", vt)
    sc.activation(rs, ssq, AF.Ln, bias=np.float32(LN_EPS))
    sc.drain()
    st = p.inc(sc.activation(rs, rs, AF.Exp, scale=np.float32(-0.5)), "act")
    p.wait(v, "act", st)

    tp_done = [0, 0]
    ev_done = [0, 0]
    last_ev = 0
    for i in range(NT):
        slot = i % 2
        xt = xres[:, i * 1024:(i + 1) * 1024]
        stage = xln_tok[:, slot * 1024:(slot + 1) * 1024]
        p.wait(v, "mm", tp_done[slot])
        if DEBUG_LN_BYPASS:
            ntok = p.inc(v.tensor_copy(stage, xt), "vec")
        else:
            ntok = p.inc(v.tensor_scalar(stage, xt, nm[:, i:i + 1], rs[:, i:i + 1],
                                         ALU.add, ALU.mult), "vec")
        p.wait(te, "vec", max(ntok, last_ev))
        ins = None
        for f in range(FCH):
            ins = te.transpose(tp_ps[:, f * 128:(f + 1) * 128],
                               stage[:, f * 128:(f + 1) * 128], ident[:, :])
        tp_done[slot] = p.inc(ins, "mm")
        p.wait(v, "mm", tp_done[slot])
        dstv = xlnT[:, :].rearrange("p (f t) -> p f t", f=FCH)[:, :, i * 128:(i + 1) * 128]
        srcv = tp_ps[:, :].rearrange("p (f t) -> p f t", f=FCH)
        last_ev = p.inc(v.tensor_copy(dstv, srcv), "vec")
    return last_ev


def emit_l1ffn_prologue(p, w, bufs):
    nc = p.nc
    v, te, g, sy = nc.vector, nc.tensor, nc.gpsimd, nc.sync
    out = {}
    g1sb = bufs["tt_g1"]
    g2sb = bufs["tt_g2"]
    mstage = bufs["tt_mstage"]
    m6sb = bufs["tt_m6"]
    ps = bufs["tt_ps"]
    ps_t = bufs["tt_ps_t"]
    ident = bufs["ident"]
    mdram = bufs["tt_mdram"]
    m6dram = bufs["tt_m6dram"]

    for idx, (g1, g2) in enumerate([(w["L1_ffn_G1_R"], w["L1_ffn_G2_R"]),
                                    (w["L1_ffn_G1_I"], w["L1_ffn_G2_I"])]):
        name = "wr" if idx == 0 else "wi"
        p.barrier([te, v, sy, g])
        g1v = bass.AP(g1.tensor, 0, [[32, 4], [128, 32], [1, 32]])
        g2v = bass.AP(g2.tensor, 0, [[256, 4], [16, 16], [1, 16]])
        p.dma(g, "ttg", g1sb[0:4, idx * 1024:(idx + 1) * 1024]
              .rearrange("p (b d) -> p b d", b=32), g1v)
        p.dma(g, "ttg", g2sb[0:4, idx * 256:(idx + 1) * 256]
              .rearrange("p (e g) -> p e g", e=16), g2v)
        p.wait_dma(te, "ttg")
        for ch in range(8):
            p.wait(te, "vec", p.cnt("vec"))
            mt = p.inc(te.matmul(ps[:, 0:256],
                                 g1sb[0:4, idx * 1024 + ch * 128: idx * 1024 + (ch + 1) * 128],
                                 g2sb[0:4, idx * 256:(idx + 1) * 256],
                                 start=True, stop=True), "mm")
            p.wait(v, "mm", mt)
            p.wait_dma(v, "ttm")
            et = p.inc(v.tensor_copy(mstage[:, :], ps[:, 0:256]), "vec")
            p.wait(sy, "vec", et)
            p.dma(sy, "ttm", mdram[ch * 128:(ch + 1) * 128, :], mstage[:, :])
        p.wait_dma(sy, "ttm")
        for b_ in range(32):
            p.dma(sy, "ttb", bass.AP(m6dram, b_ * 16 * 512, [[512, 16], [16, 32], [1, 16]]),
                  bass.AP(mdram, b_ * 32 * 256, [[16, 16], [256, 32], [1, 16]]))
        p.wait_dma(sy, "ttb")
        p.dma(sy, "ttm6", m6sb[:, :].rearrange("p (c n) -> p c n", c=4),
              bass.AP(m6dram, 0, [[512, 128], [128 * 512, 4], [1, 512]]))
        p.wait_dma(te, "ttm6")
        wtile = bufs[f"tt_{name}"]
        for r in range(4):
            p.wait(te, "vec", p.cnt("vec"))
            ins = None
            for c in range(4):
                ins = te.matmul(ps_t[:, c * 128:(c + 1) * 128],
                                m6sb[:, c * 512 + r * 128: c * 512 + (r + 1) * 128],
                                ident[:, :], start=True, stop=True, is_transpose=True)
            tt = p.inc(ins, "mm")
            p.wait(v, "mm", tt)
            p.inc(v.tensor_copy(wtile[:, r * 512:(r + 1) * 512], ps_t[:, 0:512]), "vec")
        out[name] = wtile
    wneg = bufs["tt_wni"]
    p.wait(v, "vec", p.cnt("vec"))
    nt = p.inc(v.tensor_scalar(wneg[:, :], out["wi"][:, :], -1.0, None, ALU.mult), "vec")
    out["wni"] = wneg
    brow = bufs["tt_bias"]
    p.dma(g, "ttbias", brow[0:1, 0:512], w["L1_ffn_bias_real"].rearrange("(o d) -> o d", o=1))
    p.dma(g, "ttbias", brow[0:1, 512:1024], w["L1_ffn_bias_imag"].rearrange("(o d) -> o d", o=1))
    out["ready_vec"] = nt
    out["bias"] = brow
    return out


def emit_l1ffn(p, xres, xlnT, wctx, bufs):
    nc = p.nc
    v, te = nc.vector, nc.tensor
    ones_row = bufs["ones_row"]
    yps = bufs["y_ps"]
    wr, wni, wi = wctx["wr"], wctx["wni"], wctx["wi"]
    brow = wctx["bias"]
    p.wait(te, "vec", wctx["ready_vec"])
    p.wait_dma(te, "ttbias")
    evac = [0, 0]
    for i in range(NT):
        slot = i % 2
        ps = yps[:, slot * 1024:(slot + 1) * 1024]
        p.wait(te, "vec", evac[slot])
        mt = 0
        for half, (w_lo, w_hi, boff) in enumerate([(wr, wni, 0), (wi, wr, 512)]):
            dst = ps[:, half * 512:(half + 1) * 512]
            for fc in range(4):
                te.matmul(dst, xlnT[:, xlnT_col(fc, i * 128, 128)],
                          w_lo[:, fc * 512:(fc + 1) * 512], start=(fc == 0), stop=False)
            for fc in range(4):
                te.matmul(dst, xlnT[:, xlnT_col(4 + fc, i * 128, 128)],
                          w_hi[:, fc * 512:(fc + 1) * 512], start=False, stop=False)
            mt = p.inc(te.matmul(dst, ones_row[0:1, :], brow[0:1, boff:boff + 512],
                                 start=False, stop=True), "mm")
        p.wait(v, "mm", mt)
        xt = xres[:, i * 1024:(i + 1) * 1024]
        evac[slot] = p.inc(v.tensor_tensor(xt, ps, xt, ALU.add), "vec")
    return evac



def alloc_moe(p):
    mb = {}
    mb["gw"] = p.sbuf("moe_gw", [128, 64], BF16)
    mb["b1T"] = p.sbuf("moe_b1T", [128, 32], F32)
    mb["bs1T"] = p.sbuf("moe_bs1T", [128, 4], F32)
    mb["b2row"] = p.sbuf("moe_b2row", [8, 1024], BF16)
    mb["bs2row"] = p.sbuf("moe_bs2row", [1, 1024], BF16)
    mb["ws1"] = p.sbuf("moe_ws1", [128, 4096], BF16)
    mb["ws2"] = p.sbuf("moe_ws2", [128, 4096], BF16)
    mb["rs_col"] = p.sbuf("moe_rs_col", [128, 1], F32)
    mb["rs_sb"] = p.sbuf("moe_rs_sb", [1, 1], BF16)
    mb["scores"] = p.sbuf("moe_scores", [128, 128], F32)
    mb["gtmp"] = p.sbuf("moe_gtmp", [128, 384], F32)
    mb["gstat"] = p.sbuf("moe_gstat", [128, 64], F32)
    mb["gbf"] = p.sbuf("moe_gbf", [128, 128], BF16)
    mb["gT"] = p.sbuf("moe_gT", [8, 2048], BF16)
    mb["grows"] = p.sbuf("moe_grows", [1, 16384], BF16)
    mb["grep"] = p.sbuf("moe_grep", [128, 2048], BF16)
    mb["we1"] = [p.sbuf("moe_we1_0", [128, 4096], BF16), p.sbuf("moe_we1_1", [128, 4096], BF16)]
    mb["we2"] = [p.sbuf("moe_we2_0", [128, 4096], BF16), p.sbuf("moe_we2_1", [128, 4096], BF16)]
    mb["hg"] = [p.sbuf("moe_hg_0", [128, 1024], BF16), p.sbuf("moe_hg_1", [128, 1024], BF16)]
    return mb


def emit_moe_prologue(p, w, bufs, mb):
    nc = p.nc
    g, te, v = nc.gpsimd, nc.tensor, nc.vector
    gwt = w["L0_ffn_gate_W"].tensor
    p.dma(g, "moe_c", mb["gw"][:, :].rearrange("p (f e) -> p f e", f=FCH),
          bass.AP(gwt, 0, [[8, 128], [1024, 8], [1, 8]]))
    for e in range(8):
        for c in range(4):
            p.dma(g, "moe_c", mb["b1T"][:, e * 4 + c:e * 4 + c + 1],
                  w["L0_ffn_b1"][e, c * 128:(c + 1) * 128].rearrange("(q n) -> q n", n=1))
    for c in range(4):
        p.dma(g, "moe_c", mb["bs1T"][:, c:c + 1],
              w["L0_ffn_bs1"][c * 128:(c + 1) * 128].rearrange("(q n) -> q n", n=1))
    p.dma(g, "moe_c", mb["b2row"][:, :], w["L0_ffn_b2"])
    p.dma(g, "moe_c", mb["bs2row"][:, :], w["L0_ffn_bs2"].rearrange("(o n) -> o n", o=1))
    ws1t = w["L0_ffn_Ws1"].tensor
    p.dma(g, "moe_c", mb["ws1"][:, :].rearrange("p (f h) -> p f h", f=FCH),
          bass.AP(ws1t, 0, [[512, 128], [128 * 512, 8], [1, 512]]))
    ws2t = w["L0_ffn_Ws2"].tensor
    p.dma(g, "moe_c", mb["ws2"][:, :].rearrange("p (c d) -> p c d", c=4),
          bass.AP(ws2t, 0, [[1024, 128], [128 * 1024, 4], [1, 1024]]))
    p.dma(g, "moe_c", mb["rs_sb"][:, :], w["L0_ffn_route_scale"])
    p.wait_dma(te, "moe_c")
    p.wait_dma(te, "const")
    mt = p.inc(te.matmul(bufs["ps_d"][:, 0:1], bufs["ones_row"][0:1, :], mb["rs_sb"][0:1, 0:1],
                         start=True, stop=True), "mm")
    v = nc.vector
    p.wait(v, "mm", mt)
    p.inc(v.tensor_copy(mb["rs_col"][:, :], bufs["ps_d"][:, 0:1]), "vec")


def emit_moe_weight_load(p, w, mb, e, slot):
    g = p.nc.gpsimd
    w1t = w["L0_ffn_W1"].tensor
    w2t = w["L0_ffn_W2"].tensor
    p.dma(g, "moe_w%d" % slot, mb["we1"][slot][:, :].rearrange("p (f h) -> p f h", f=FCH),
          bass.AP(w1t, e * 1024 * 512, [[512, 128], [128 * 512, 8], [1, 512]]))
    p.dma(g, "moe_w%d" % slot, mb["we2"][slot][:, :].rearrange("p (c d) -> p c d", c=4),
          bass.AP(w2t, e * 512 * 1024, [[1024, 128], [128 * 1024, 4], [1, 1024]]))


def emit_moe(p, w, xres, xlnT, bufs, mb, callidx):
    nc = p.nc
    v, sc, te, g = nc.vector, nc.scalar, nc.tensor, nc.gpsimd
    ps_a, ps_b, ps_c, ps_d = bufs["ps_a"], bufs["ps_b"], bufs["ps_c"], bufs["ps_d"]
    ones = bufs["ones_row"]

    p.wait(te, "vec", p.cnt("vec"))
    p.wait(te, "act", p.cnt("act"))
    gt_mm = 0
    for i in range(NT):
        for fc in range(FCH):
            ins = te.matmul(ps_d[:, i * 8:(i + 1) * 8],
                            xlnT[:, xlnT_col(fc, i * 128, 128)],
                            mb["gw"][:, fc * 8:(fc + 1) * 8],
                            start=(fc == 0), stop=(fc == FCH - 1))
        gt_mm = p.inc(ins, "mm")
    p.wait(sc, "mm", gt_mm)
    st = p.inc(sc.activation(mb["scores"][:, :], ps_d[:, 0:128], AF.Sigmoid), "act")
    p.wait(v, "act", st)
    sv = mb["scores"][:, :].rearrange("p (i e) -> p i e", e=8)
    m1 = mb["gstat"][:, 0:16]
    m2 = mb["gstat"][:, 16:32]
    recip = mb["gstat"][:, 32:48]
    eq = mb["gtmp"][:, 0:128].rearrange("p (i e) -> p i e", e=8)
    sc2 = mb["gtmp"][:, 128:256].rearrange("p (i e) -> p i e", e=8)
    gg = mb["gtmp"][:, 256:384]
    p.inc(v.tensor_scalar(mb["scores"][:, :], mb["scores"][:, :], mb["rs_col"][:, 0:1],
                          None, ALU.mult), "vec")
    v.drain()
    p.inc(v.tensor_reduce(m1, sv, AX.X, ALU.max), "vec")
    v.drain()
    m1b = m1.rearrange("p (i o) -> p i o", o=1).broadcast_to((128, 16, 8))
    p.inc(v.tensor_tensor(eq, sv, m1b, ALU.is_ge), "vec")
    p.inc(v.scalar_tensor_tensor(sc2, eq, -1e30, sv, ALU.mult, ALU.add), "vec")
    v.drain()
    p.inc(v.tensor_reduce(m2, sc2, AX.X, ALU.max), "vec")
    v.drain()
    p.inc(v.tensor_tensor(mb["gstat"][:, 48:64], m1, m2, ALU.add), "vec")
    v.drain()
    p.inc(v.reciprocal(recip, mb["gstat"][:, 48:64]), "vec")
    m2b = m2.rearrange("p (i o) -> p i o", o=1).broadcast_to((128, 16, 8))
    p.inc(v.tensor_tensor(eq, sv, m2b, ALU.is_ge), "vec")
    v.drain()
    recb = recip.rearrange("p (i o) -> p i o", o=1).broadcast_to((128, 16, 8))
    ggv = gg.rearrange("p (i e) -> p i e", e=8)
    p.inc(v.tensor_tensor(ggv, sv, eq, ALU.mult), "vec")
    v.drain()
    p.inc(v.tensor_tensor(ggv, ggv, recb, ALU.mult), "vec")
    v.drain()
    gv = p.inc(v.tensor_copy(mb["gbf"][:, :], gg), "vec")
    p.wait(te, "vec", gv)
    for rnd in range(2):
        ins = None
        for j in range(8):
            i = rnd * 8 + j
            ins = te.matmul(ps_c[0:8, j * 128:(j + 1) * 128], mb["gbf"][:, i * 8:(i + 1) * 8],
                            bufs["ident"][:, :], start=True, stop=True, is_transpose=True)
        tt = p.inc(ins, "mm")
        p.wait(v, "mm", tt)
        gv = p.inc(v.tensor_copy(mb["gT"][0:8, rnd * 1024:(rnd + 1) * 1024],
                                 ps_c[0:8, 0:1024]), "vec")
        if rnd == 0:
            p.wait(te, "vec", gv)
    p.wait(g, "vec", gv)
    p.dma(g, "moe_gr%d" % callidx, mb["grows"][0:1, :].rearrange("o (e t) -> o e t", e=8),
          mb["gT"][:, :])

    emit_moe_weight_load(p, w, mb, 0, 0)
    y_prev_tok = 0
    evac_tok = 0
    for c8 in range(8):
        p.wait_dma(te, "moe_gr%d" % callidx)
        p.wait(te, "vec", p.cnt("vec"))
        ins = None
        for e in range(8):
            ins = te.matmul(ps_a[:, e * 256:(e + 1) * 256], ones[0:1, :],
                            mb["grows"][0:1, e * 2048 + c8 * 256: e * 2048 + (c8 + 1) * 256],
                            start=True, stop=True)
        tt = p.inc(ins, "mm")
        p.wait(v, "mm", tt)
        grt = p.inc(v.tensor_copy(mb["grep"][:, :], ps_a[:, :]), "vec")
        p.wait(te, "vec", grt)
        for e in range(9):
            shared = (e == 8)
            slot = e % 2
            if not shared:
                p.wait_dma(te, "moe_w%d" % slot)
            w1 = mb["ws1"] if shared else mb["we1"][slot]
            w2 = mb["ws2"] if shared else mb["we2"][slot]
            p.wait(te, "act", p.cnt("act"))
            p.wait(te, "vec", p.cnt("vec"))
            ins = None
            for hc in range(4):
                for fc in range(FCH):
                    ins = te.matmul(ps_b[:, hc * 256:(hc + 1) * 256],
                                    w1[:, fc * 512 + hc * 128: fc * 512 + (hc + 1) * 128],
                                    xlnT[:, xlnT_col(fc, c8 * 256, 256)],
                                    start=(fc == 0), stop=(fc == FCH - 1))
            hmm = p.inc(ins, "mm")
            if not shared and not (c8 == 7 and e == 7):
                p.wait(g, "mm", y_prev_tok)
                emit_moe_weight_load(p, w, mb, (e + 1) % 8, 1 - slot)
            hgbuf = mb["hg"][slot if not shared else 0]
            p.wait(sc, "mm", hmm)
            atok = 0
            for hc in range(4):
                bcol = mb["bs1T"][:, hc:hc + 1] if shared else mb["b1T"][:, e * 4 + hc:e * 4 + hc + 1]
                atok = p.inc(sc.activation(hgbuf[:, hc * 256:(hc + 1) * 256],
                                           ps_b[:, hc * 256:(hc + 1) * 256], AF.Gelu,
                                           bias=bcol), "act")
            if not shared:
                p.wait(v, "act", atok)
                hv = hgbuf[:, :].rearrange("p (h t) -> p h t", h=4)
                gr = mb["grep"][:, e * 256:(e + 1) * 256].rearrange("p (o t) -> p o t", o=1) \
                    .broadcast_to((128, 4, 256))
                gtok = p.inc(v.tensor_tensor(hv, hv, gr, ALU.mult), "vec")
                p.wait(te, "vec", gtok)
            else:
                p.wait(te, "act", atok)
            for dt in range(2):
                for half in range(2):
                    for hc in range(4):
                        ins = te.matmul(ps_a[:, dt * 1024 + half * 512: dt * 1024 + (half + 1) * 512],
                                        hgbuf[:, hc * 256 + dt * 128: hc * 256 + dt * 128 + 128],
                                        w2[:, hc * 1024 + half * 512: hc * 1024 + (half + 1) * 512],
                                        start=(e == 0 and hc == 0), stop=False)
            y_prev_tok = p.inc(ins, "mm")
        for dt in range(2):
            i = c8 * 2 + dt
            for half in range(2):
                dst = ps_a[:, dt * 1024 + half * 512: dt * 1024 + (half + 1) * 512]
                te.matmul(dst, mb["gT"][0:8, i * 128:(i + 1) * 128],
                          mb["b2row"][0:8, half * 512:(half + 1) * 512], start=False, stop=False)
                ins = te.matmul(dst, ones[0:1, :], mb["bs2row"][0:1, half * 512:(half + 1) * 512],
                                start=False, stop=True)
        ytok = p.inc(ins, "mm")
        p.wait(v, "mm", ytok)
        for dt in range(2):
            i = c8 * 2 + dt
            xt = xres[:, i * 1024:(i + 1) * 1024]
            evac_tok = p.inc(v.tensor_tensor(xt, ps_a[:, dt * 1024:(dt + 1) * 1024], xt,
                                             ALU.add), "vec")
    return evac_tok


def alloc_common(p, need_tt=False):
    bufs = {}
    bufs["ident"] = p.sbuf("ident", [128, 128], BF16)
    bufs["ones_row"] = p.sbuf("ones_row", [1, 128], BF16)
    bufs["stats"] = p.sbuf("stats", [128, 96], F32)
    bufs["sq_dump"] = p.sbuf("sq_dump", [128, 1024], F32)
    bufs["xln_tok"] = p.sbuf("xln_tok", [128, 2048], BF16)
    bufs["ps_a"] = p.psum("ps_a", [128, 2048], F32)
    bufs["ps_b"] = p.psum("ps_b", [128, 1024], F32)
    bufs["ps_c"] = p.psum("ps_c", [128, 1024], BF16)
    bufs["ps_d"] = p.psum("ps_d", [128, 512], F32)
    bufs["tp_ps"] = bufs["ps_c"]
    if need_tt:
        bufs["tt_g1"] = p.sbuf("tt_g1", [4, 2048], BF16)
        bufs["tt_g2"] = p.sbuf("tt_g2", [4, 512], BF16)
        bufs["tt_mstage"] = p.sbuf("tt_mstage", [128, 256], BF16)
        bufs["tt_m6"] = p.sbuf("tt_m6", [128, 2048], BF16)
        bufs["tt_wr"] = p.sbuf("tt_wr", [128, 2048], BF16)
        bufs["tt_wi"] = p.sbuf("tt_wi", [128, 2048], BF16)
        bufs["tt_wni"] = p.sbuf("tt_wni", [128, 2048], BF16)
        bufs["tt_bias"] = p.sbuf("tt_bias", [1, 1024], BF16)
        bufs["tt_ps"] = bufs["ps_d"]
        bufs["tt_ps_t"] = bufs["ps_c"]
        bufs["tt_mdram"] = p.dram("tt_mdram", [1024, 256], BF16)
        bufs["tt_m6dram"] = p.dram("tt_m6dram", [512, 512], BF16)
        bufs["y_ps"] = bufs["ps_a"]
    return bufs


def const_inputs():
    return dict(
        c_ident=np.eye(128, dtype=np.float32),
        c_ones=np.ones((1, 128), np.float32),
    )


def emit_const_loads(p, bufs, dins):
    g = p.nc.gpsimd
    p.dma(g, "const", bufs["ident"][:, :], dins["c_ident"])
    p.dma(g, "const", bufs["ones_row"][:, :], dins["c_ones"])




MOE_WNAMES = ["L0_ffn_gate_W", "L0_ffn_W1", "L0_ffn_b1", "L0_ffn_W2", "L0_ffn_b2",
              "L0_ffn_Ws1", "L0_ffn_bs1", "L0_ffn_Ws2", "L0_ffn_bs2", "L0_ffn_route_scale"]


def build_moe_program():
    shapes = dict(L0_ffn_gate_W=[1024, 8], L0_ffn_W1=[8, 1024, 512], L0_ffn_b1=[8, 512],
                  L0_ffn_W2=[8, 512, 1024], L0_ffn_b2=[8, 1024], L0_ffn_Ws1=[1024, 512],
                  L0_ffn_bs1=[512], L0_ffn_Ws2=[512, 1024], L0_ffn_bs2=[1024],
                  L0_ffn_route_scale=[1, 1])
    nc = bass.Bass("TRN2", target_bir_lowering=False)
    x_in = nc.dram_tensor("x_in", [TOK, D], F32, kind="ExternalInput").ap()
    x_out = nc.dram_tensor("x_out", [TOK, D], F32, kind="ExternalOutput").ap()
    g_out = nc.dram_tensor("g_out", [128, 128], F32, kind="ExternalOutput").ap()
    dins = {}
    for nm, shp in shapes.items():
        dins[nm] = nc.dram_tensor(nm, list(shp), F32, kind="ExternalInput").ap()
    for nm, shp in [("c_ident", [128, 128]), ("c_ones", [1, 128])]:
        dins[nm] = nc.dram_tensor(nm, shp, F32, kind="ExternalInput").ap()
    with ExitStack() as ctx:
        p = Prog(nc, ctx)
        xres = p.sbuf("xres", [128, NT * 1024], F32)
        xlnT = p.sbuf("xlnT", [128, FCH * TOK], BF16)
        bufs = alloc_common(p, need_tt=False)
        mb = alloc_moe(p)
        sy = nc.sync
        emit_const_loads(p, bufs, dins)
        p.dma(sy, "x", xres[:, :].rearrange("p (n f) -> p n f", n=NT),
              x_in.rearrange("(n p) f -> p n f", p=128))
        emit_moe_prologue(p, dins, bufs, mb)
        p.wait_dma(nc.vector, "x")
        p.wait_dma(nc.scalar, "x")
        p.wait_dma(nc.tensor, "const")
        emit_ln_xlnt(p, xres, xlnT, bufs, "s0")
        ev = emit_moe(p, dins, xres, xlnT, bufs, mb, 0)
        p.wait(sy, "vec", ev)
        p.dma(sy, "gdump", g_out, mb["gtmp"][:, 256:384])
        p.dma(sy, "xout", x_out.rearrange("(n p) f -> p n f", p=128),
              xres[:, :].rearrange("p (n f) -> p n f", n=NT))
        p.wait_dma(sy, "xout")
    return nc


def build_l1ffn_program(weight_shapes):
    nc = bass.Bass("TRN2", target_bir_lowering=False)
    x_in = nc.dram_tensor("x_in", [TOK, D], F32, kind="ExternalInput").ap()
    x_out = nc.dram_tensor("x_out", [TOK, D], F32, kind="ExternalOutput").ap()
    dins = {}
    for nm, shp in weight_shapes.items():
        dins[nm] = nc.dram_tensor(nm, list(shp), F32, kind="ExternalInput").ap()
    for nm in ["c_ident", "c_ones"]:
        shp = {"c_ident": [128, 128], "c_ones": [1, 128]}[nm]
        dins[nm] = nc.dram_tensor(nm, shp, F32, kind="ExternalInput").ap()

    with ExitStack() as ctx:
        p = Prog(nc, ctx)
        xres = p.sbuf("xres", [128, NT * 1024], F32)
        xlnT = p.sbuf("xlnT", [128, FCH * TOK], BF16)
        bufs = alloc_common(p, need_tt=True)
        sy = nc.sync
        emit_const_loads(p, bufs, dins)
        p.dma(sy, "x", xres[:, :].rearrange("p (n f) -> p n f", n=NT),
              x_in.rearrange("(n p) f -> p n f", p=128))
        wctx = emit_l1ffn_prologue(p, dins, bufs)
        p.wait_dma(nc.vector, "x")
        p.wait_dma(nc.scalar, "x")
        p.wait_dma(nc.tensor, "const")
        emit_ln_xlnt(p, xres, xlnT, bufs, "s0")
        evac = emit_l1ffn(p, xres, xlnT, wctx, bufs)
        p.wait(sy, "vec", max(evac))
        p.dma(sy, "xout", x_out.rearrange("(n p) f -> p n f", p=128),
              xres[:, :].rearrange("p (n f) -> p n f", n=NT))
        p.wait_dma(sy, "xout")
    return nc



L0A = dict(name="l0a", heads=16, dh=64, hpr=2, kind="full", rope=None,
           qscale=1.0 / 32.0, alibi=False)
L0B = dict(name="l0b", heads=16, dh=64, hpr=2, kind="anti", rope=None,
           qscale=1.0 / 8.0, alibi=True)
L1A = dict(name="l1a", heads=8, dh=128, hpr=1, kind="causal", rope="l1a",
           qscale=1.0 / 128.0, alibi=False)
L1B = dict(name="l1b", heads=16, dh=64, hpr=2, kind="causal", rope="l1b",
           qscale=1.0 / 8.0, alibi=False)

ALIBI_SLOPES = [2.0 ** (-0.5 * (i + 1)) for i in range(16)]


def attn_schedule(kind, i0):
    out = []
    if kind == "full":
        for base in (0, 16):
            for j in range(16):
                out.append((base + j, 0, 512, None))
    elif kind == "causal":
        for base in (0, 16):
            for j in range(min(i0 + 4, 16)):
                qa = max(0, (j - i0) * 128)
                diag = j - i0 if i0 <= j <= i0 + 3 else None
                out.append((base + j, qa, 512, diag))
    else:
        for base in (0, 16):
            for j in range(i0, 16):
                qb = min(512, (j - i0 + 1) * 128)
                diag = j - i0 if j <= i0 + 3 else None
                out.append((base + j, 0, qb, diag))
    return out


def alloc_attn(p, cfg):
    ab = {}
    ab["wbufA"] = p.sbuf("at_wbufA", [128, 8192], BF16)
    ab["woutb"] = ab["wbufA"]
    ab["wsm"] = p.sbuf("at_wsm", [128, 320], BF16)
    ab["stage"] = p.sbuf("at_stage", [128, 2048], BF16)
    ab["ropetmp"] = p.sbuf("at_ropetmp", [128, 1024], F32)
    ab["ksb"] = [p.sbuf("at_ksb0", [128, 4096], BF16), p.sbuf("at_ksb1", [128, 4096], BF16)]
    ab["qsb"] = [p.sbuf("at_qsb0", [128, 512], BF16), p.sbuf("at_qsb1", [128, 512], BF16)]
    ab["vsb"] = [p.sbuf("at_vsb0", [128, 2080], BF16), p.sbuf("at_vsb1", [128, 2080], BF16)]
    ab["psb"] = p.sbuf("at_psb", [128, 2048], BF16)
    ab["osb"] = p.sbuf("at_osb", [128, 4096], BF16)
    ab["otst"] = p.sbuf("at_otst", [128, 1024], BF16)
    ab["rec"] = p.sbuf("at_rec", [128, 64], F32)
    ab["bout"] = p.sbuf("at_bout", [1, 1024], BF16)
    if cfg["kind"] != "full":
        ab["masks"] = p.sbuf("at_masks", [128, 256], BF16)
    if cfg["alibi"]:
        ab["alibi"] = p.sbuf("at_alibi", [128, 1024], F32)
    if cfg["rope"]:
        ab["cosq"] = p.sbuf("at_cosq", [128, 2048], BF16)
        ab["sinq"] = p.sbuf("at_sinq", [128, 2048], BF16)
        ab["coskb"] = p.sbuf("at_coskb", [128, 2048], BF16)
        ab["sinkb"] = p.sbuf("at_sinkb", [128, 2048], BF16)
    ab["q_dram"] = p.dram("at_qdram", [1024, 2048], BF16)
    ab["k_dram"] = p.dram("at_kdram", [1024, 4096], BF16)
    ab["v_dram"] = p.dram("at_vdram", [4096, 1024], BF16)
    return ab


def emit_attn_tables(p, cfg, ab, tins, tag):
    g = p.nc.gpsimd
    st = "atab_" + tag
    if cfg["kind"] != "full":
        p.dma(g, st, ab["masks"][:, :], tins[f"{tag}_masks"])
    if cfg["alibi"]:
        p.dma(g, st, ab["alibi"][:, :], tins[f"{tag}_alibi"])
    if cfg["rope"]:
        p.dma(g, st, ab["cosq"][:, :], tins[f"{tag}_cosq"])
        p.dma(g, st, ab["sinq"][:, :], tins[f"{tag}_sinq"])
    return st


def emit_attn_ktables(p, cfg, ab, tins, tag, phase):
    g = p.nc.gpsimd
    st = f"aktab_{tag}{phase}"
    p.dma(g, st, ab["coskb"][:, :], tins[f"{tag}_cosk{phase}"])
    p.dma(g, st, ab["sinkb"][:, :], tins[f"{tag}_sink{phase}"])
    return st


def _stage_slot(ab, sl):
    return ab["stage"][:, sl * 1024:(sl + 1) * 1024]


def _evac_plain(p, ab, sc, ps, colrange, stage_sl, scale, tok):
    p.wait(sc, "mm", tok)
    p.wait_dma(sc, f"apj{stage_sl}")
    return p.inc(sc.activation(_stage_slot(ab, stage_sl)[:, colrange], ps[:, colrange],
                               AF.Identity, scale=np.float32(scale)), "act")


def emit_attn_proj_one(p, w, cfg, xlnT, bufs, ab, tag, phase, slotbase, which):
    nc = p.nc
    v, sc, te, g, sy = nc.vector, nc.scalar, nc.tensor, nc.gpsimd, nc.sync
    ps_d, ps_b = bufs["ps_d"], bufs["ps_b"]
    ps_od = bufs["ps_a"][:, 1536:2048]
    name = cfg["name"]
    scale = cfg["qscale"] if which == "q" else 1.0

    def evac_plain(ps, colrange, stage_sl, sc_, tok):
        return _evac_plain(p, ab, sc, ps, colrange, stage_sl, sc_, tok)

    dst = ab["q_dram"] if which == "q" else ab["k_dram"]
    colbase0 = 0 if which == "q" else slotbase * 128
    if name == "l0a":
        wcol = 0 if which == "q" else 64
        for rt in range(8):
            for ch in range(4):
                sl = ch % 2
                p.wait(te, "act", p.cnt("act"))
                rhsA = xlnT[0:64, xlnT_col(rt, ch * 512, 512)]
                rhsB = xlnT[64:128, xlnT_col(rt, ch * 512, 512)]
                te.matmul(ps_d[0:64, 0:512], ab["wsm"][0:64, wcol:wcol + 64], rhsA,
                          start=True, stop=True, tile_position=(0, 0))
                mt = p.inc(te.matmul(ps_d[64:128, 0:512], ab["wsm"][64:128, wcol:wcol + 64], rhsB,
                                     start=True, stop=True, tile_position=(64, 64)), "mm")
                at = evac_plain(ps_d, slice(0, 512), sl, scale, mt)
                p.wait(sy, "act", at)
                p.dma(sy, f"apj{sl}",
                      dst[rt * 128:(rt + 1) * 128, colbase0 + ch * 512:colbase0 + ch * 512 + 512],
                      _stage_slot(ab, sl)[:, 0:512])
    elif name == "l0b":
        wb = ab["wbufA"]
        for rt in range(8):
            for ch in range(4):
                sl = ch % 2
                p.wait(te, "act", p.cnt("act"))
                ins = None
                for fc in range(FCH):
                    ins = te.matmul(ps_d[:, 0:512],
                                    wb[:, fc * 1024 + rt * 128: fc * 1024 + (rt + 1) * 128],
                                    xlnT[:, xlnT_col(fc, ch * 512, 512)],
                                    start=(fc == 0), stop=(fc == FCH - 1))
                mt = p.inc(ins, "mm")
                at = evac_plain(ps_d, slice(0, 512), sl, scale, mt)
                p.wait(sy, "act", at)
                p.dma(sy, f"apj{sl}",
                      dst[rt * 128:(rt + 1) * 128, colbase0 + ch * 512:colbase0 + ch * 512 + 512],
                      _stage_slot(ab, sl)[:, 0:512])
    else:
        wb = ab["wbufA"]
        cos = ab["cosq"] if which == "q" else ab["coskb"]
        sin = ab["sinq"] if which == "q" else ab["sinkb"]
        n_ev_tiles = 2 if name == "l1a" else 4
        n_skip = 4 if name == "l1a" else 0
        dr = dst.shape[1]
        hstride = 128 if name == "l1a" else 64
        hper = 4
        for pr in range(n_ev_tiles):
            evrow = pr
            odrow = n_ev_tiles + pr
            for ch in range(4):
                sl = ch % 2
                p.wait(te, "act", p.cnt("act"))
                p.wait(te, "vec", p.cnt("vec"))
                ins = None
                for fc in range(FCH):
                    te.matmul(ps_d[:, 0:512],
                              wb[:, fc * 1024 + evrow * 128: fc * 1024 + (evrow + 1) * 128],
                              xlnT[:, xlnT_col(fc, ch * 512, 512)],
                              start=(fc == 0), stop=(fc == FCH - 1))
                    ins = te.matmul(ps_od[:, 0:512],
                                    wb[:, fc * 1024 + odrow * 128: fc * 1024 + (odrow + 1) * 128],
                                    xlnT[:, xlnT_col(fc, ch * 512, 512)],
                                    start=(fc == 0), stop=(fc == FCH - 1))
                mt = p.inc(ins, "mm")
                p.wait(v, "mm", mt)
                p.wait_dma(v, f"apj{sl}")
                C = cos[:, ch * 512:(ch + 1) * 512]
                S = sin[:, ch * 512:(ch + 1) * 512]
                t1 = ab["ropetmp"][:, 0:512]
                t2 = ab["ropetmp"][:, 512:1024]
                stg = _stage_slot(ab, sl)
                v.tensor_tensor(t1, ps_d[:, 0:512], C, ALU.mult)
                v.tensor_tensor(t2, ps_od[:, 0:512], S, ALU.mult)
                v.drain()
                v.tensor_tensor(stg[:, 0:512], t1, t2, ALU.subtract)
                v.tensor_tensor(t1, ps_d[:, 0:512], S, ALU.mult)
                v.tensor_tensor(t2, ps_od[:, 0:512], C, ALU.mult)
                v.drain()
                vt = p.inc(v.tensor_tensor(stg[:, 512:1024], t1, t2, ALU.add), "vec")
                p.wait(sy, "vec", vt)
                for half, rowoff in [(0, 0), (1, 32)]:
                    for hh in range(4):
                        p.dma(sy, f"apj{sl}",
                              bass.AP(dst,
                                      ((hper * pr + hh) * hstride + rowoff) * dr
                                      + colbase0 + ch * 512,
                                      [[dr, 32], [1, 512]]),
                              _stage_slot(ab, sl)[hh * 32:(hh + 1) * 32,
                                                  half * 512:(half + 1) * 512])
        for sk in range(n_skip):
            for ch in range(4):
                sl = ch % 2
                p.wait(te, "act", p.cnt("act"))
                p.wait(te, "vec", p.cnt("vec"))
                ins = None
                for fc in range(FCH):
                    ins = te.matmul(ps_d[:, 0:512],
                                    wb[:, fc * 1024 + (2 * n_ev_tiles + sk) * 128:
                                       fc * 1024 + (2 * n_ev_tiles + sk + 1) * 128],
                                    xlnT[:, xlnT_col(fc, ch * 512, 512)],
                                    start=(fc == 0), stop=(fc == FCH - 1))
                mt = p.inc(ins, "mm")
                at = evac_plain(ps_d, slice(0, 512), sl, scale, mt)
                p.wait(sy, "act", at)
                for hh in range(2):
                    p.dma(sy, f"apj{sl}",
                          bass.AP(dst, ((2 * sk + hh) * 128 + 64) * dr + colbase0 + ch * 512,
                                  [[dr, 64], [1, 512]]),
                          _stage_slot(ab, sl)[hh * 64:(hh + 1) * 64, 0:512])


def emit_attn_proj_v(p, w, cfg, xlnT, bufs, ab, tag, phase, slotbase):
    nc = p.nc
    v, sc, te, g, sy = nc.vector, nc.scalar, nc.tensor, nc.gpsimd, nc.sync
    ps_d, ps_b = bufs["ps_d"], bufs["ps_b"]
    name = cfg["name"]

    def evac_plain(ps, colrange, stage_sl, scale, tok):
        return _evac_plain(p, ab, sc, ps, colrange, stage_sl, scale, tok)

    p.wait(te, "gp", p.cnt("gp"))
    for j in range(NT):
        sl = j % 2
        p.wait(te, "act", p.cnt("act"))
        if name == "l0a":
            ins = None
            for rt in range(8):
                lhs = xlnT[:, xlnT_col(rt, j * 128, 128)]
                te.matmul(ps_b[:, (2 * rt) * 64:(2 * rt + 1) * 64], lhs,
                          ab["wsm"][:, 192:256], start=True, stop=True)
                ins = te.matmul(ps_b[:, (2 * rt + 1) * 64:(2 * rt + 2) * 64], lhs,
                                ab["wsm"][:, 256:320], start=True, stop=True)
            mt = p.inc(ins, "mm")
        else:
            wb = ab["wbufA"]
            off = 0
            ins = None
            for half in range(2):
                for fc in range(FCH):
                    ins = te.matmul(ps_b[:, half * 512:(half + 1) * 512],
                                    xlnT[:, xlnT_col(fc, j * 128, 128)],
                                    wb[:, off + fc * 1024 + half * 512: off + fc * 1024 + (half + 1) * 512],
                                    start=(fc == 0), stop=(fc == FCH - 1))
            mt = p.inc(ins, "mm")
        at = evac_plain(ps_b, slice(0, 1024), sl, 1.0, mt)
        p.wait(sy, "act", at)
        p.dma(sy, f"apj{sl}", ab["v_dram"][(slotbase + j) * 128:(slotbase + j + 1) * 128, :],
              _stage_slot(ab, sl)[:, 0:1024])


def emit_attn_sweep(p, w, cfg, xres, bufs, ab, tag):
    nc = p.nc
    v, sc, te, g, sy = nc.vector, nc.scalar, nc.tensor, nc.gpsimd, nc.sync
    ps_a, ps_b, ps_c, ps_d = bufs["ps_a"], bufs["ps_b"], bufs["ps_c"], bufs["ps_d"]
    name, kind = cfg["name"], cfg["kind"]
    hpr = cfg["hpr"]
    dh = cfg["dh"]
    dv1 = dh + 1
    n_rt = cfg["heads"] // hpr
    ones = bufs["ones_row"]

    ksb, qsb, vsb, psb = ab["ksb"], ab["qsb"], ab["vsb"], ab["psb"]
    p.inc(g.memset(vsb[0][:, :], 1.0), "gp")
    p.inc(g.memset(vsb[1][:, :], 1.0), "gp")
    p.wait(te, "gp", p.cnt("gp"))
    vfc = 0
    for qc in range(4):
        i0 = qc * 4
        sched = attn_schedule(kind, i0)
        p.wait(te, "vec", p.cnt("vec"))
        for rt in range(n_rt):
            rsl = rt % 2
            p.wait(g, "mm", p.cnt("mm"))
            p.dma(g, f"ak{rsl}", ksb[rsl][:, :], ab["k_dram"][rt * 128:(rt + 1) * 128, :])
            p.dma(g, f"aq{rsl}", qsb[rsl][:, :],
                  ab["q_dram"][rt * 128:(rt + 1) * 128, qc * 512:(qc + 1) * 512])
            p.wait_dma(te, f"ak{rsl}")
            p.wait_dma(te, f"aq{rsl}")
            for hl in range(hpr):
                h = rt * hpr + hl
                kslice = ksb[rsl][hl * dh:(hl + 1) * dh, :]
                qslice = qsb[rsl][hl * dh:(hl + 1) * dh, :]
                started = set()
                sset = 0
                exp_tok = [0, 0]
                pv_tok = [0, 0]
                for si, (s, qa, qb, diag) in enumerate(sched):
                    vsl = vfc % 2
                    vfc += 1
                    p.wait(g, "mm", pv_tok[vsl])
                    p.dma(g, f"av{vsl}",
                          vsb[vsl][:, 0:dv1].rearrange("q (o e) -> q o e", o=1)[:, :, 0:dh],
                          ab["v_dram"][s * 128:(s + 1) * 128, h * dh:(h + 1) * dh]
                          .rearrange("q (o e) -> q o e", o=1))
                    sps = ps_d[:, 0:512] if sset == 0 else ps_b[:, 0:512]
                    p.wait(te, "act", exp_tok[sset])
                    smt = p.inc(te.matmul(sps[:, qa:qb], kslice[:, s * 128:(s + 1) * 128],
                                          qslice[:, qa:qb], start=True, stop=True), "mm")
                    p.wait(sc, "mm", smt)
                    p.wait(sc, "vec", p.cnt("vec"))
                    pdst = psb[:, (si % 2) * 512:(si % 2) * 512 + 512]
                    if cfg["alibi"]:
                        osel = 0 if s < 16 else 1
                        j = s if s < 16 else s - 16
                        for sigma in range(4):
                            ca = max(qa, sigma * 128)
                            cb = min(qb, (sigma + 1) * 128)
                            if ca >= cb:
                                continue
                            delta = j - (i0 + sigma)
                            col0 = osel * 512 + h * 32 + delta * 2
                            if h >= 2:
                                p.inc(sc.activation(pdst[:, ca:cb], sps[:, ca:cb], AF.Exp,
                                                    bias=ab["alibi"][:, col0:col0 + 1]), "act")
                            else:
                                for sub2 in range(2):
                                    p.inc(sc.activation(
                                        pdst[:, ca + sub2 * 64:ca + (sub2 + 1) * 64],
                                        sps[:, ca + sub2 * 64:ca + (sub2 + 1) * 64],
                                        AF.Exp,
                                        bias=ab["alibi"][:, col0 + sub2:col0 + sub2 + 1]), "act")
                    else:
                        p.inc(sc.activation(pdst[:, qa:qb], sps[:, qa:qb], AF.Exp), "act")
                    exp_tok[sset] = p.cnt("act")
                    if diag is not None:
                        mt = ab["masks"][:, 0:128] if s < 16 else ab["masks"][:, 128:256]
                        p.wait(v, "act", exp_tok[sset])
                        p.inc(v.tensor_tensor(pdst[:, diag * 128:(diag + 1) * 128],
                                              pdst[:, diag * 128:(diag + 1) * 128], mt,
                                              ALU.mult), "vec")
                    p.wait_dma(te, f"av{vsl}")
                    if diag is not None:
                        p.wait(te, "vec", p.cnt("vec"))
                    else:
                        p.wait(te, "act", exp_tok[sset])
                    ins = None
                    for sub in range(qa // 128, (qb + 127) // 128):
                        ins = te.matmul(ps_a[:, sub * 512:sub * 512 + dv1],
                                        pdst[:, sub * 128:(sub + 1) * 128],
                                        vsb[vsl][:, 0:dv1],
                                        start=sub not in started, stop=False)
                        started.add(sub)
                    pv_tok[vsl] = p.inc(ins, "mm")
                    sset = 1 - sset
                allpv = p.cnt("mm")
                p.wait(v, "mm", allpv)
                for sub in range(4):
                    p.inc(v.reciprocal(ab["rec"][:, sub:sub + 1],
                                       ps_a[:, sub * 512 + dh:sub * 512 + dh + 1]), "vec")
                v.drain()
                for sub in range(4):
                    p.inc(v.tensor_scalar(
                        ab["osb"][:, sub * 1024 + h * dh: sub * 1024 + (h + 1) * dh],
                        ps_a[:, sub * 512:sub * 512 + dh],
                        ab["rec"][:, sub:sub + 1], None, ALU.mult), "vec")
                p.wait(te, "vec", p.cnt("vec"))
        for sub in range(4):
            i = i0 + sub
            otile = ab["osb"][:, sub * 1024:(sub + 1) * 1024]
            p.wait(te, "vec", p.cnt("vec"))
            ins = None
            for chn in range(8):
                ins = te.matmul(ps_c[:, chn * 128:(chn + 1) * 128],
                                otile[:, chn * 128:(chn + 1) * 128], bufs["ident"][:, :],
                                start=True, stop=True, is_transpose=True)
            tt = p.inc(ins, "mm")
            p.wait(v, "mm", tt)
            ot = p.inc(v.tensor_copy(ab["otst"][:, :], ps_c[:, 0:1024]), "vec")
            p.wait(te, "vec", ot)
            ins = None
            for half in range(2):
                dst = ps_b[:, half * 512:(half + 1) * 512]
                for chn in range(8):
                    te.matmul(dst, ab["otst"][:, chn * 128:(chn + 1) * 128],
                              ab["woutb"][:, chn * 1024 + half * 512: chn * 1024 + (half + 1) * 512],
                              start=(chn == 0), stop=False)
                ins = te.matmul(dst, ones[0:1, :], ab["bout"][0:1, half * 512:(half + 1) * 512],
                                start=False, stop=True)
            yt = p.inc(ins, "mm")
            p.wait(v, "mm", yt)
            xt = xres[:, i * 1024:(i + 1) * 1024]
            p.inc(v.tensor_tensor(xt, ps_b[:, 0:1024], xt, ALU.add), "vec")
    return p.cnt("vec")


def _load_fcmajor(p, g, stream, dstbuf, src_tensor, src_off, ncols, src_rowstride):
    p.dma(g, stream, dstbuf[:, 0:FCH * ncols].rearrange("p (f c) -> p f c", f=FCH),
          bass.AP(src_tensor, src_off,
                  [[src_rowstride, 128], [128 * src_rowstride, FCH], [1, ncols]]))


def emit_attn_sublayer(p, w, cfg, xres, xlnT, bufs, ab, tag, tins, xsib_in, bufs_ln):
    nc = p.nc
    v, sc, te, g, sy = nc.vector, nc.scalar, nc.tensor, nc.gpsimd, nc.sync
    name = cfg["name"]
    tabst = emit_attn_tables(p, cfg, ab, tins, tag)
    stq = f"aw_{tag}"

    if name == "l0a":
        for ci, wn in enumerate(["L0_attn_a_Wq_t", "L0_attn_a_Wk_t", "L0_attn_a_Wv_t"]):
            for base in (0, 64):
                p.dma(g, stq, ab["wsm"][base:base + 64, ci * 64:(ci + 1) * 64], w[wn])
        p.inc(g.memset(ab["wsm"][:, 192:320], 0.0), "gp")
        p.dma(g, stq, ab["wsm"][0:64, 192:256], w["L0_attn_a_Wv_t"])
        p.dma(g, stq, ab["wsm"][64:128, 256:320], w["L0_attn_a_Wv_t"])
        p.dma(g, stq, ab["bout"][0:1, :], w["L0_attn_a_b_out"].rearrange("(o n) -> o n", o=1))
    else:
        p.inc(g.memset(ab["bout"][0:1, :], 0.0), "gp")

    def loadw(src_name, off, rowstride):
        p.wait(g, "mm", p.cnt("mm"))
        _load_fcmajor(p, g, stq, ab["wbufA"], w[src_name].tensor, off, 1024, rowstride)
        p.wait_dma(te, stq)

    phases = [(0, 16), (1, 0)] if ATTN_STAGE >= 3 else [(1, 0)]
    for phase, slotbase in phases:
        if phase == 0:
            p.dma(sy, "xsib_" + tag, xres[:, :].rearrange("p (n f) -> p n f", n=NT),
                  xsib_in.rearrange("(n p) f -> p n f", p=128))
            for eng in (v, sc):
                p.wait_dma(eng, "xsib_" + tag)
        else:
            p.wait(sy, "vec", p.cnt("vec"))
            p.wait(sy, "act", p.cnt("act"))
            p.dma(sy, "xown_" + tag, xres[:, :].rearrange("p (n f) -> p n f", n=NT),
                  w["__x_own"].rearrange("(n p) f -> p n f", p=128))
            for eng in (v, sc, sy):
                p.wait_dma(eng, "xown_" + tag)
        if cfg["rope"]:
            p.wait(g, "vec", p.cnt("vec"))
            ktabst = emit_attn_ktables(p, cfg, ab, tins, tag, phase)
            p.wait_dma(v, ktabst)
        p.wait_dma(te, tabst)
        p.wait_dma(v, tabst)
        p.wait_dma(sc, tabst)
        p.wait_dma(te, stq)
        emit_ln_xlnt(p, xres, xlnT, bufs_ln, tag + f"ph{phase}")
        p.wait(te, "vec", p.cnt("vec"))
        if ATTN_STAGE == 0:
            continue
        if name == "l0b":
            loadw("L0_attn_b_W_qkv", 1024, 3072)
        elif name == "l1a":
            loadw("L1_attn_a_wk_g", 0, 1024)
        elif name == "l1b":
            loadw("L1_attn_b_WK_g", 0, 1024)
        emit_attn_proj_one(p, w, cfg, xlnT, bufs, ab, tag, phase, slotbase, "k")
        if ATTN_STAGE == 1:
            continue
        if phase == 1:
            if name == "l0b":
                loadw("L0_attn_b_W_qkv", 0, 3072)
            elif name == "l1a":
                loadw("L1_attn_a_wq_g", 0, 1024)
            elif name == "l1b":
                loadw("L1_attn_b_WQ_g", 0, 1024)
            emit_attn_proj_one(p, w, cfg, xlnT, bufs, ab, tag, phase, 0, "q")
        if ATTN_STAGE == 15:
            continue
        if name == "l0b":
            loadw("L0_attn_b_W_qkv", 2048, 3072)
        elif name == "l1a":
            loadw("L1_attn_a_wv_g", 0, 1024)
        elif name == "l1b":
            loadw("L1_attn_b_WV_t", 0, 1024)
        emit_attn_proj_v(p, w, cfg, xlnT, bufs, ab, tag, phase, slotbase)
        if ATTN_STAGE in (2, 16):
            continue
    wout_name = {"l0a": "L0_attn_a_W_out_t", "l0b": "L0_attn_b_W_out",
                 "l1a": "L1_attn_a_wo", "l1b": "L1_attn_b_WO_t"}[name]
    loadw(wout_name, 0, 1024)
    for eng in (te, sc, v, g):
        p.wait_dma(eng, "apj0")
        p.wait_dma(eng, "apj1")
    p.wait(te, "gp", p.cnt("gp"))
    p.wait_dma(te, stq)
    if ATTN_STAGE < 4:
        return p.cnt("vec")
    return emit_attn_sweep(p, w, cfg, xres, bufs, ab, tag)


def attn_host_weights(name, w):
    o = {}
    if name == "l0a":
        o["L0_attn_a_Wq_t"] = np.ascontiguousarray(np.asarray(w["L0_attn_a_Wq"]).T)
        o["L0_attn_a_Wk_t"] = np.ascontiguousarray(np.asarray(w["L0_attn_a_Wk"]).T)
        o["L0_attn_a_Wv_t"] = np.ascontiguousarray(np.asarray(w["L0_attn_a_Wv"]).T)
        o["L0_attn_a_W_out_t"] = np.ascontiguousarray(np.asarray(w["L0_attn_a_W_out"]).T)
        o["L0_attn_a_b_out"] = np.asarray(w["L0_attn_a_b_out"])
    elif name == "l0b":
        o["L0_attn_b_W_qkv"] = np.asarray(w["L0_attn_b_W_qkv"])
        o["L0_attn_b_W_out"] = np.asarray(w["L0_attn_b_W_out"])
    elif name == "l1a":
        wq = np.asarray(w["L1_attn_a_wq"])
        wk = np.asarray(w["L1_attn_a_wk"])
        wv = np.asarray(w["L1_attn_a_wv"])
        perm = ([h * 128 + j for h in range(8) for j in range(32)] +
                [h * 128 + 32 + j for h in range(8) for j in range(32)] +
                [h * 128 + 64 + j for h in range(8) for j in range(64)])
        allw = lambda a: np.concatenate([a[h] for h in range(8)], axis=1)
        o["L1_attn_a_wq_g"] = np.ascontiguousarray(allw(wq)[:, perm])
        o["L1_attn_a_wk_g"] = np.ascontiguousarray(allw(wk)[:, perm])
        o["L1_attn_a_wv_g"] = np.ascontiguousarray(allw(wv))
        o["L1_attn_a_wo"] = np.asarray(w["L1_attn_a_wo"])
    elif name == "l1b":
        perm = ([h * 64 + 2 * j for h in range(16) for j in range(32)] +
                [h * 64 + 2 * j + 1 for h in range(16) for j in range(32)])
        o["L1_attn_b_WQ_g"] = np.ascontiguousarray(np.asarray(w["L1_attn_b_WQ"]).T[:, perm])
        o["L1_attn_b_WK_g"] = np.ascontiguousarray(np.asarray(w["L1_attn_b_WK"]).T[:, perm])
        o["L1_attn_b_WV_t"] = np.ascontiguousarray(np.asarray(w["L1_attn_b_WV"]).T)
        o["L1_attn_b_WO_t"] = np.ascontiguousarray(np.asarray(w["L1_attn_b_WO"]).T)
    return o


ATTN_WNAMES = {
    "l0a": ["L0_attn_a_Wq_t", "L0_attn_a_Wk_t", "L0_attn_a_Wv_t", "L0_attn_a_W_out_t",
            "L0_attn_a_b_out"],
    "l0b": ["L0_attn_b_W_qkv", "L0_attn_b_W_out"],
    "l1a": ["L1_attn_a_wq_g", "L1_attn_a_wk_g", "L1_attn_a_wv_g", "L1_attn_a_wo"],
    "l1b": ["L1_attn_b_WQ_g", "L1_attn_b_WK_g", "L1_attn_b_WV_t", "L1_attn_b_WO_t"],
}
ATTN_WSHAPES = {
    "L0_attn_a_Wq_t": [64, 64], "L0_attn_a_Wk_t": [64, 64], "L0_attn_a_Wv_t": [64, 64],
    "L0_attn_a_W_out_t": [1024, 1024], "L0_attn_a_b_out": [1024],
    "L0_attn_b_W_qkv": [1024, 3072], "L0_attn_b_W_out": [1024, 1024],
    "L1_attn_a_wq_g": [1024, 1024], "L1_attn_a_wk_g": [1024, 1024],
    "L1_attn_a_wv_g": [1024, 1024], "L1_attn_a_wo": [1024, 1024],
    "L1_attn_b_WQ_g": [1024, 1024], "L1_attn_b_WK_g": [1024, 1024],
    "L1_attn_b_WV_t": [1024, 1024], "L1_attn_b_WO_t": [1024, 1024],
}


def attn_host_tables(cfg, h_core):
    name, kind = cfg["name"], cfg["kind"]
    tag = name
    o = {}
    cgrid, pgrid = np.meshgrid(np.arange(128), np.arange(128), indexing="xy")
    pp = np.arange(128)[:, None]
    cc = np.arange(128)[None, :]
    if kind == "causal":
        own = (cc >= pp).astype(np.float32)
        sib = np.full((128, 128), 1.0 if h_core == 1 else 0.0, np.float32)
        o[f"{tag}_masks"] = np.concatenate([own, sib], 1)
    elif kind == "anti":
        own = (cc <= pp).astype(np.float32)
        sib = np.full((128, 128), 1.0 if h_core == 0 else 0.0, np.float32)
        o[f"{tag}_masks"] = np.concatenate([own, sib], 1)
    if cfg["alibi"]:
        tab = np.zeros((128, 1024), np.float32)
        for osel in range(2):
            for h in range(16):
                for delta in range(16):
                    dprime = 2 * delta if osel == 0 else 2 * delta + 1 - 2 * h_core
                    for sub in range(2):
                        col = osel * 512 + h * 32 + delta * 2 + sub
                        if dprime < 0:
                            tab[:, col] = -1e30
                        else:
                            tab[:, col] = -ALIBI_SLOPES[h] * (128.0 * dprime + np.arange(128) - 64.0 * sub)
        o[f"{tag}_alibi"] = tab
    if cfg["rope"]:
        t = np.arange(TOK)
        pos_own = (2 * (t // 128) + h_core) * 128 + (t % 128)
        pos_sib = (2 * (t // 128) + (1 - h_core)) * 128 + (t % 128)
        j = np.arange(32)
        if name == "l1a":
            freq = 10000.0 ** (-j / 32.0)
        else:
            freq = 1.0 / (10000.0 ** (2.0 * j / 64.0))
        rows = np.tile(freq, 4)
        for ph, pos in [(0, pos_sib), (1, pos_own)]:
            rad = pos[None, :] * rows[:, None]
            o[f"{tag}_cosk{ph}"] = np.cos(rad).astype(np.float32)
            o[f"{tag}_sink{ph}"] = np.sin(rad).astype(np.float32)
        radq = pos_own[None, :] * rows[:, None]
        o[f"{tag}_cosq"] = (np.cos(radq) * cfg["qscale"]).astype(np.float32)
        o[f"{tag}_sinq"] = (np.sin(radq) * cfg["qscale"]).astype(np.float32)
    return o


def attn_table_shapes(cfg):
    tag = cfg["name"]
    o = {}
    if cfg["kind"] != "full":
        o[f"{tag}_masks"] = [128, 256]
    if cfg["alibi"]:
        o[f"{tag}_alibi"] = [128, 1024]
    if cfg["rope"]:
        for ph in range(2):
            o[f"{tag}_cosk{ph}"] = [128, 2048]
            o[f"{tag}_sink{ph}"] = [128, 2048]
        o[f"{tag}_cosq"] = [128, 2048]
        o[f"{tag}_sinq"] = [128, 2048]
    return o


def build_attn_program(cfg, debug_dump=False):
    nc = bass.Bass("TRN2", target_bir_lowering=False)
    x_in = nc.dram_tensor("x_in", [TOK, D], F32, kind="ExternalInput").ap()
    x_sib = nc.dram_tensor("x_sib", [TOK, D], F32, kind="ExternalInput").ap()
    x_out = nc.dram_tensor("x_out", [TOK, D], F32, kind="ExternalOutput").ap()
    if debug_dump:
        q_out = nc.dram_tensor("q_out", [1024, 2048], BF16, kind="ExternalOutput").ap()
        k_out = nc.dram_tensor("k_out", [1024, 4096], BF16, kind="ExternalOutput").ap()
        v_out = nc.dram_tensor("v_out", [4096, 1024], BF16, kind="ExternalOutput").ap()
        global DEBUG_DUMP_OSB
        DEBUG_DUMP_OSB = dict(
            osb_out=nc.dram_tensor("osb_out", [128, 4096], BF16, kind="ExternalOutput").ap(),
            rec_out=nc.dram_tensor("rec_out", [128, 64], F32, kind="ExternalOutput").ap(),
            ops_out=nc.dram_tensor("ops_out", [128, 1024], BF16, kind="ExternalOutput").ap(),
            msk_out=nc.dram_tensor("msk_out", [128, 256], BF16, kind="ExternalOutput").ap(),
            ptil_out=nc.dram_tensor("ptil_out", [128, 512], BF16, kind="ExternalOutput").ap(),
            s_out=nc.dram_tensor("s_out", [128, 512], F32, kind="ExternalOutput").ap(),
            pv0_out=nc.dram_tensor("pv0_out", [128, 256], BF16, kind="ExternalOutput").ap())
    dins = {}
    for nm in ATTN_WNAMES[cfg["name"]]:
        dins[nm] = nc.dram_tensor(nm, ATTN_WSHAPES[nm], F32, kind="ExternalInput").ap()
    for nm, shp in attn_table_shapes(cfg).items():
        dins[nm] = nc.dram_tensor(nm, shp, F32, kind="ExternalInput").ap()
    for nm, shp in [("c_ident", [128, 128]), ("c_ones", [1, 128])]:
        dins[nm] = nc.dram_tensor(nm, shp, F32, kind="ExternalInput").ap()
    dins["__x_own"] = x_in
    with ExitStack() as ctx:
        p = Prog(nc, ctx)
        xres = p.sbuf("xres", [128, NT * 1024], F32)
        xlnT = p.sbuf("xlnT", [128, FCH * TOK], BF16)
        bufs = alloc_common(p, need_tt=False)
        ab = alloc_attn(p, cfg)
        sy = nc.sync
        emit_const_loads(p, bufs, dins)
        p.wait_dma(nc.tensor, "const")
        ev = emit_attn_sublayer(p, dins, cfg, xres, xlnT, bufs, ab, cfg["name"], dins, x_sib, bufs)
        p.wait(sy, "vec", ev)
        p.dma(sy, "xout", x_out.rearrange("(n p) f -> p n f", p=128),
              xres[:, :].rearrange("p (n f) -> p n f", n=NT))
        if debug_dump:
            p.dma(sy, "xout", q_out, ab["q_dram"].ap())
            p.dma(sy, "xout", k_out, ab["k_dram"].ap())
            p.dma(sy, "xout", v_out, ab["v_dram"].ap())
            p.wait_dma(sy, "dbgosb")
        p.wait_dma(sy, "xout")
    return nc


_PROG_CACHE = {}


def _get_prog(kind):
    if kind not in _PROG_CACHE:
        if kind == "moe":
            _PROG_CACHE[kind] = build_moe_program()
        elif kind == "l1ffn":
            shapes = dict(L1_ffn_G1_R=[1, 32, 4, 32], L1_ffn_G2_R=[4, 16, 1, 16],
                          L1_ffn_G1_I=[1, 32, 4, 32], L1_ffn_G2_I=[4, 16, 1, 16],
                          L1_ffn_bias_real=[512], L1_ffn_bias_imag=[512])
            _PROG_CACHE[kind] = build_l1ffn_program(shapes)
        else:
            cfg = {"l0a": L0A, "l0b": L0B, "l1a": L1A, "l1b": L1B}[kind]
            _PROG_CACHE[kind] = build_attn_program(cfg)
    return _PROG_CACHE[kind]


SEQ = ["l0a", "moe", "l0b", "moe", "l1a", "l1ffn", "l1b", "l1ffn"]
ATTN_CFG_BY_KIND = {"l0a": L0A, "l0b": L0B, "l1a": L1A, "l1b": L1B}


def kernel(**inputs):
    w = {k: np.ascontiguousarray(np.asarray(v, np.float32)) for k, v in inputs.items()}
    x = w["L0_attn_a_x"]
    wprep = {}
    for nm in ["l0a", "l0b", "l1a", "l1b"]:
        wprep.update(attn_host_weights(nm, w))
    consts = const_inputs()
    tabs = {h: {} for h in range(2)}
    for h in range(2):
        for cfg in [L0A, L0B, L1A, L1B]:
            tabs[h].update(attn_host_tables(cfg, h))
    moe_w = {nm: w[nm] for nm in MOE_WNAMES if nm != "L0_ffn_route_scale"}
    moe_w["L0_ffn_route_scale"] = w["L0_ffn_route_scale"].reshape(1, 1)
    tt_names = ["L1_ffn_G1_R", "L1_ffn_G2_R", "L1_ffn_G1_I", "L1_ffn_G2_I",
                "L1_ffn_bias_real", "L1_ffn_bias_imag"]

    xcur = x
    for kind in SEQ:
        nc = _get_prog(kind)
        in_maps = []
        for c in range(NCORES):
            b, h = c // 2, c % 2
            m = dict(x_in=shard_x(xcur[b], h))
            if kind in ATTN_CFG_BY_KIND:
                m["x_sib"] = shard_x(xcur[b], 1 - h)
                cfg = ATTN_CFG_BY_KIND[kind]
                for nm in ATTN_WNAMES[kind]:
                    m[nm] = wprep[nm]
                for nm in attn_table_shapes(cfg):
                    m[nm] = tabs[h][nm]
            elif kind == "moe":
                m.update(moe_w)
            else:
                for nm in tt_names:
                    m[nm] = w[nm]
            m.update(consts)
            in_maps.append(m)
        res = run_bass_kernel_spmd(nc, in_maps, core_ids=list(range(NCORES)))
        xnew = np.empty_like(x)
        for b in range(4):
            xnew[b] = unshard_x(res.results[2 * b]["x_out"], res.results[2 * b + 1]["x_out"])
        xcur = xnew
    return np.ascontiguousarray(xcur)
